# version 02
# baseline (speedup 1.0000x reference)
import math
from contextlib import ExitStack

import numpy as np
import ml_dtypes

import concourse.bass as bass
import concourse.mybir as mybir
from concourse.bass_utils import run_bass_kernel_spmd

F32 = mybir.dt.float32
BF16 = mybir.dt.bfloat16
U32 = mybir.dt.uint32
AF = mybir.ActivationFunctionType
OP = mybir.AluOpType
AX = mybir.AxisListType
bf16 = ml_dtypes.bfloat16

B, S, D = 4, 4096, 1024
NH, DH = 16, 64
NCH = D // 128
SPAN0 = 2304
QS0 = [0, 1792]
NT0 = SPAN0 // 128
SPAN1 = 2048
QS1 = [0, 2048]
NT1 = SPAN1 // 128
BIG = 1.0e30
WIN = 256

PE = mybir.EngineType.PE
DEBUG_L0A = False
DEBUG_B = False
DEBUG_BW = 1



def _bucket(rel):
    nb = 16
    n = -rel
    ret = (n < 0).astype(np.int32) * nb
    n = np.abs(n)
    max_exact = nb // 2
    is_small = n < max_exact
    val_if_large = max_exact + (
        np.log(np.maximum(n, 1).astype(np.float32) / max_exact)
        / math.log(128 / max_exact) * (nb - max_exact)
    ).astype(np.int32)
    val_if_large = np.minimum(val_if_large, nb - 1)
    return ret + np.where(is_small, n, val_if_large)


def host_bias_l0(rel_emb, h):
    d = np.arange(-(S - 1), S)
    vec = rel_emb[_bucket(d), :]
    j = np.arange((NT0 - 1) * 128 + S)
    p = np.arange(128)
    idx = np.clip(j[None, :] + 1919 - QS0[h] - p[:, None], 0, 2 * S - 2)
    return np.ascontiguousarray(vec[idx].transpose(2, 0, 1))


def host_bias_l1(table, h):
    u = np.arange(2 * S - 1)
    vec = table[np.clip(2 * S - 2 - u, 0, table.shape[0] - 1), :]
    j = np.arange((NT1 - 1) * 128 + S)
    p = np.arange(128)
    idx = np.clip(j[None, :] + 2175 - QS1[h] - p[:, None], 0, 2 * S - 2)
    return np.ascontiguousarray(vec[idx].transpose(2, 0, 1))



class Sem:
    def __init__(self, ctx, nc, name):
        self.h = ctx.enter_context(nc.semaphore(name))
        self.n = 0


class Prog:

    ENGS = ("sync", "gpsimd", "tensor", "vector", "scalar")

    def __init__(self, nc, ctx):
        self.nc = nc
        self.ctx = ctx
        self.q = {e: [] for e in self.ENGS}

    def sem(self, name):
        return Sem(self.ctx, self.nc, name)

    def on(self, eng, fn, sem=None, k=1):
        if sem is not None:
            sem.n += k

            def wrapped(e, fn=fn, sem=sem, k=k):
                fn(e).then_inc(sem.h, k)
            self.q[eng].append(wrapped)
        else:
            self.q[eng].append(lambda e, fn=fn: fn(e))

    def wait(self, eng, sem, val=None):
        v = sem.n if val is None else val
        if v > 0:
            self.q[eng].append(lambda e, s=sem, v=v: e.wait_ge(s.h, v))

    def finalize(self, block):
        for eng in self.ENGS:
            ops = self.q[eng]

            def body(e, ops=ops):
                for fn in ops:
                    fn(e)
            getattr(block, eng)(body)


def dma(prog, eng, out, in_, sem):
    prog.on(eng, lambda e, o=out, i=in_: e.dma_start(out=o, in_=i), sem, 16)


def emit_ln_transpose(prog, x_src, n_rows, xn_fm, bufs, sems, col0=0):
    (xt, bns, mv, sd, rstd, negmur, xn_tm, ident, ptr, epsb) = bufs
    (ldx, mm, cv, ac) = sems
    xt_free = [0, 0]
    for rt in range(n_rows // 128):
        buf = rt % 2
        prog.wait("sync", cv, xt_free[buf])
        dma(prog, "sync", xt[:, buf, :], x_src[rt * 128:(rt + 1) * 128, :], ldx)
        prog.wait("vector", ldx)
        prog.wait("vector", mm)
        prog.on("vector", lambda e, b=buf: e.bn_stats(bns[:, 0:6], xt[:, b, 0:512]))
        prog.on("vector", lambda e, b=buf: e.bn_stats(bns[:, 6:12], xt[:, b, 512:1024]), cv)
        prog.wait("vector", cv)
        prog.on("vector", lambda e: e.bn_aggr(mv[:], bns[:]), cv)
        prog.wait("scalar", cv)
        prog.on("scalar", lambda e: e.activation(out=sd[:], in_=mv[:, 1:2],
                                                 func=AF.Sqrt, bias=epsb[:], scale=1.0), ac)
        prog.wait("vector", ac)
        prog.on("vector", lambda e: e.reciprocal(rstd[:], sd[:]), cv)
        prog.wait("vector", cv)
        prog.on("vector", lambda e: e.tensor_scalar(
            out=negmur[:], in0=mv[:, 0:1], scalar1=rstd[:], scalar2=-1.0,
            op0=OP.mult, op1=OP.mult), cv)
        prog.wait("vector", cv)
        prog.on("vector", lambda e, b=buf: e.tensor_scalar(
            out=xn_tm[:], in0=xt[:, b, :], scalar1=rstd[:], scalar2=negmur[:],
            op0=OP.mult, op1=OP.add), cv)
        xt_free[buf] = cv.n
        prog.wait("tensor", cv)
        for ch in range(NCH):
            prog.on("tensor", lambda e, c=ch: e.transpose(
                ptr[:, c * 128:(c + 1) * 128], xn_tm[:, c * 128:(c + 1) * 128],
                ident[:]), mm if ch == NCH - 1 else None)
        prog.wait("vector", mm)
        prog.on("vector", lambda e, rt=rt: e.tensor_copy(
            xn_fm[:, :, col0 + rt * 128: col0 + (rt + 1) * 128],
            ptr.ap().rearrange("p (c f) -> p c f", c=NCH)), cv)



def build_l0_attn(qoff):
    nc = bass.Bass("TRN2", target_bir_lowering=False)
    ctx = ExitStack()
    dbg = {}
    if DEBUG_L0A:
        for nm, shp in [("d_K", [128, S]), ("d_Q", [128, SPAN0]),
                        ("d_V", [128, S // 128, 128]), ("d_sneg", [128, S]),
                        ("d_negmax", [128, 1]), ("d_P", [128, S]),
                        ("d_sums", [128, 1]), ("d_ctx", [128, 128]),
                        ("d_EB", [128, 2, 6272])]:
            dt = F32 if nm in ("d_sneg", "d_negmax", "d_sums") else F32
            dbg[nm] = nc.dram_tensor(nm, shp, dt, kind="ExternalOutput").ap()

    x0 = nc.dram_tensor("x0", [S, D], F32, kind="ExternalInput").ap()
    wq = nc.dram_tensor("wq", [D, D], F32, kind="ExternalInput").ap()
    wk = nc.dram_tensor("wk", [D, D], F32, kind="ExternalInput").ap()
    wv = nc.dram_tensor("wv", [D, D], F32, kind="ExternalInput").ap()
    wo = nc.dram_tensor("wo", [D, D], F32, kind="ExternalInput").ap()
    bias0 = nc.dram_tensor("bias0", [16, 128, 6272], F32, kind="ExternalInput").ap()
    ident_in = nc.dram_tensor("ident_in", [128, 128], F32, kind="ExternalInput").ap()
    s0_out = nc.dram_tensor("s0_out", [SPAN0, D], F32, kind="ExternalOutput").ap()
    xn_out = nc.dram_tensor("xn_out", [128, NCH, S], F32, kind="ExternalOutput").ap()

    sb = lambda n, shp, dt: ctx.enter_context(nc.sbuf_tensor(n, shp, dt))
    psum = lambda n, shp, dt: ctx.enter_context(nc.psum_tensor(n, shp, dt))

    xn_fm = sb("xn_fm", [128, NCH, S], BF16)
    ident = sb("ident", [128, 128], BF16)
    epsb = sb("epsb", [128, 1], F32)
    w_hp = sb("w_hp", [128, 3, NCH, 128], BF16)
    Kb = sb("Kb", [128, S], BF16)
    Vtm = sb("Vtm", [128, S // 128, 128], BF16)
    Qb = sb("Qb", [128, SPAN0], BF16)
    EB = sb("EB", [128, 2, 6272], BF16)
    sneg = sb("sneg", [128, S], F32)
    scr = sb("scr", [128, 512], F32)
    nmstrip = sb("nmstrip", [128, 8], F32)
    negmax = sb("negmax", [128, 1], F32)
    Pb = sb("Pb", [128, S], BF16)
    PT = sb("PT", [128, S // 128, 128], BF16)
    sums = sb("sums", [128, 1], F32)
    sums2 = sb("sums2", [128, 1], F32)
    nmstrip2 = sb("nmstrip2", [128, 8], F32)
    rsum = sb("rsum", [128, 1], F32)
    ctxn = sb("ctxn", [128, NCH, SPAN0], BF16)
    s0st = sb("s0st", [128, D], F32)
    ictx = ExitStack()
    sbi = lambda n, shp, dt: ictx.enter_context(nc.sbuf_tensor(n, shp, dt))
    xt = sbi("xt", [128, 2, D], F32)
    bns = sbi("bns", [128, 12], F32)
    mv = sbi("mv", [128, 2], F32)
    sd = sbi("sd", [128, 1], F32)
    rstd = sbi("rstd", [128, 1], F32)
    negmur = sbi("negmur", [128, 1], F32)
    xn_tm = sbi("xn_tm", [128, D], BF16)

    pscore = [psum(f"pscore{i}", [128, 512], F32) for i in range(4)]
    ptr = [psum(f"ptr{i}", [128, 1024], BF16) for i in range(2)]
    pctx = psum("pctx", [128, 128], F32)
    pv2 = psum("pv2", [128, 128], F32)

    prog = Prog(nc, ctx)
    ldw = prog.sem("ldw")
    ldx = prog.sem("ldx")
    ldb = prog.sem("ldb")
    mm = prog.sem("mm")
    cv = prog.sem("cv")
    ac = prog.sem("ac")
    pl = prog.sem("pl")
    st = prog.sem("st")

    dma(prog, "gpsimd", ident[:], ident_in, ldw)
    prog.on("gpsimd", lambda e: e.memset(epsb[:], 1e-5), pl)
    prog.wait("scalar", pl)
    prog.wait("tensor", ldw, 16)

    emit_ln_transpose(prog, x0, S, xn_fm,
                      (xt, bns, mv, sd, rstd, negmur, xn_tm, ident, ptr[0], epsb),
                      (ldx, mm, cv, ac))
    phase1_cv = cv.n
    ictx.close()
    wo_sb = sb("wo_sb", [128, NCH, D], BF16)
    prog.wait("gpsimd", cv, phase1_cv)
    dma(prog, "gpsimd", wo_sb[:], wo.rearrange("(c p) m -> p c m", p=128), ldw)

    NKT = S // 512
    bank_free = [0, 0, 0, 0]
    ptr_free = [0, 0]
    pb_free = 0
    for hp in range(NH // 2):
        hA, hB = 2 * hp, 2 * hp + 1
        if hp > 0:
            prog.wait("gpsimd", mm)
            prog.wait("gpsimd", cv)
        for wi, w_ap in enumerate((wq, wk, wv)):
            dma(prog, "gpsimd", w_hp[:, wi, :, :],
                w_ap.rearrange("(c p) m -> p c m", p=128)[:, :, hA * 64:(hB + 1) * 64],
                ldw)
        dma(prog, "gpsimd", EB[:, 0, :], bias0[hA], ldb)
        dma(prog, "gpsimd", EB[:, 1, :], bias0[hB], ldb)
        prog.wait("scalar", ldb)
        if hp > 0:
            prog.wait("scalar", cv)
        prog.on("scalar", lambda e: e.activation(out=EB[:, 0, :], in_=EB[:, 0, :],
                                                 func=AF.Exp), ac)
        prog.on("scalar", lambda e: e.activation(out=EB[:, 1, :], in_=EB[:, 1, :],
                                                 func=AF.Exp), ac)

        prog.wait("tensor", ldw)
        if hp == 0:
            prog.wait("tensor", cv, phase1_cv)
        for kt in range(NKT):
            bk = kt % 2
            prog.wait("tensor", cv, bank_free[bk])
            for kc in range(NCH):
                prog.on("tensor", lambda e, kt=kt, kc=kc, bk=bk: e.matmul(
                    pscore[bk][:], w_hp[:, 1, kc, :],
                    xn_fm[:, kc, kt * 512:(kt + 1) * 512],
                    start=(kc == 0), stop=(kc == NCH - 1)),
                    mm if kc == NCH - 1 else None)
            prog.wait("vector", mm)
            prog.on("vector", lambda e, kt=kt, bk=bk: e.tensor_copy(
                Kb[:, kt * 512:(kt + 1) * 512], pscore[bk][:]), cv)
            bank_free[bk] = cv.n
        NQ5 = (SPAN0 + 511) // 512
        for q5 in range(NQ5):
            w = min(512, SPAN0 - q5 * 512)
            bk = 2 + q5 % 2
            prog.wait("tensor", cv, bank_free[bk])
            for kc in range(NCH):
                prog.on("tensor", lambda e, q5=q5, kc=kc, w=w, bk=bk: e.matmul(
                    pscore[bk][:, 0:w], w_hp[:, 0, kc, :],
                    xn_fm[:, kc, qoff + q5 * 512: qoff + q5 * 512 + w],
                    start=(kc == 0), stop=(kc == NCH - 1)),
                    mm if kc == NCH - 1 else None)
            prog.wait("vector", mm)
            prog.on("vector", lambda e, q5=q5, w=w, bk=bk: e.tensor_copy(
                Qb[:, q5 * 512: q5 * 512 + w], pscore[bk][:, 0:w]), cv)
            bank_free[bk] = cv.n
        for tt in range(S // 128):
            prog.wait("tensor", cv, bank_free[tt % 2])
            for kc in range(NCH):
                prog.on("tensor", lambda e, tt=tt, kc=kc: e.matmul(
                    pscore[tt % 2][:, 0:128], xn_fm[:, kc, tt * 128:(tt + 1) * 128],
                    w_hp[:, 2, kc, :],
                    start=(kc == 0), stop=(kc == NCH - 1)),
                    mm if kc == NCH - 1 else None)
            prog.wait("vector", mm)
            prog.on("vector", lambda e, tt=tt: e.tensor_copy(
                Vtm[:, tt, :], pscore[tt % 2][:, 0:128]), cv)
            bank_free[tt % 2] = cv.n
        prog.wait("vector", ac)
        if DEBUG_L0A and hp == 0:
            prog.wait("gpsimd", cv)
            prog.wait("gpsimd", ac)
            dma(prog, "gpsimd", dbg["d_K"], Kb[:], st)
            dma(prog, "gpsimd", dbg["d_Q"], Qb[:], st)
            dma(prog, "gpsimd", dbg["d_V"], Vtm[:], st)
            dma(prog, "gpsimd", dbg["d_EB"], EB[:], st)

        for qt in range(NT0):
            for hh in range(2):
                base = hh * 64
                for kc in range(NKT):
                    bk = kc % 4
                    prog.wait("tensor", cv, bank_free[bk])
                    prog.on("tensor", lambda e, kc=kc, qt=qt, base=base, bk=bk: e.matmul(
                        pscore[bk][:], Qb[base:base + 64, qt * 128:(qt + 1) * 128],
                        Kb[base:base + 64, kc * 512:(kc + 1) * 512],
                        start=True, stop=True), mm)
                    prog.wait("vector", mm)
                    prog.on("vector", lambda e, kc=kc, bk=bk: e.tensor_scalar(
                        out=sneg[:, kc * 512:(kc + 1) * 512], in0=pscore[bk][:],
                        scalar1=-1.0, scalar2=3.0e38, op0=OP.mult, op1=OP.min,
                        accum_out=nmstrip[:, kc:kc + 1]), cv)
                    bank_free[bk] = cv.n
                prog.wait("scalar", cv)
                prog.on("scalar", lambda e: e.activation(
                    out=nmstrip2[:], in_=nmstrip[:], func=AF.Copy), ac)
                prog.wait("vector", ac)
                prog.on("vector", lambda e: e.tensor_reduce(
                    out=negmax[:], in_=nmstrip2[:], axis=AX.X, op=OP.min), cv)
                prog.wait("scalar", cv)
                prog.wait("scalar", mm, pb_free)
                prog.on("scalar", lambda e: e.activation(
                    out=Pb[:], in_=sneg[:], func=AF.Exp, bias=negmax[:],
                    scale=-1.0), ac)
                prog.wait("vector", ac)
                c0 = (NT0 - 1 - qt) * 128
                prog.on("vector", lambda e, hh=hh, c0=c0: e.scalar_tensor_tensor(
                    out=Pb[:], in0=Pb[:], scalar=1.0,
                    in1=EB[:, hh, c0:c0 + S], op0=OP.mult, op1=OP.mult,
                    accum_out=sums[:]), cv)
                prog.wait("scalar", cv)
                prog.on("scalar", lambda e: e.activation(
                    out=sums2[:], in_=sums[:], func=AF.Copy), ac)
                prog.wait("vector", ac)
                prog.on("vector", lambda e: e.reciprocal(rsum[:], sums2[:]), cv)
                prog.wait("gpsimd", cv)
                prog.on("gpsimd", lambda e: e.tensor_scalar(
                    out=Pb[:], in0=Pb[:], scalar1=rsum[:], scalar2=None,
                    op0=OP.mult), pl)
                if DEBUG_L0A and hp == 0 and qt == 0 and hh == 0:
                    prog.wait("gpsimd", pl)
                    dma(prog, "gpsimd", dbg["d_sneg"], sneg[:], st)
                    dma(prog, "gpsimd", dbg["d_negmax"], negmax[:], st)
                    dma(prog, "gpsimd", dbg["d_P"], Pb[:], st)
                    dma(prog, "gpsimd", dbg["d_sums"], sums2[:], st)
                    prog.wait("scalar", st)
                prog.wait("tensor", pl)
                for t8 in range(4):
                    pb = t8 % 2
                    prog.wait("tensor", cv, ptr_free[pb])
                    for t in range(8):
                        tt = t8 * 8 + t
                        prog.on("tensor", lambda e, tt=tt, t=t, pb=pb: e.transpose(
                            ptr[pb][:, t * 128:(t + 1) * 128],
                            Pb[:, tt * 128:(tt + 1) * 128], ident[:]),
                            mm if t == 7 else None)
                    prog.wait("vector", mm)
                    prog.on("vector", lambda e, t8=t8, pb=pb: e.tensor_copy(
                        PT[:, t8 * 8:(t8 + 1) * 8, :],
                        ptr[pb].ap().rearrange("p (c f) -> p c f", c=8)), cv)
                    ptr_free[pb] = cv.n
                pb_free = mm.n
                prog.wait("tensor", cv)
                for tt in range(S // 128):
                    prog.on("tensor", lambda e, tt=tt, base=base: e.matmul(
                        pctx[base:base + 64, :], Vtm[:, tt, base:base + 64],
                        PT[:, tt, :], start=(tt == 0), stop=(tt == S // 128 - 1)),
                        mm if tt == S // 128 - 1 else None)
            prog.wait("vector", mm)
            prog.on("vector", lambda e, qt=qt, hp=hp: e.tensor_copy(
                ctxn[:, hp, qt * 128:(qt + 1) * 128], pctx[:]), cv)
            if DEBUG_L0A and hp == 0 and qt == 0:
                prog.wait("gpsimd", cv)
                dma(prog, "gpsimd", dbg["d_ctx"], ctxn[:, 0, 0:128], st)
            prog.wait("tensor", cv)

    prog.wait("gpsimd", cv, phase1_cv)
    dma(prog, "gpsimd", xn_out, xn_fm[:], st)
    prog.wait("tensor", cv)
    for qt in range(NT0):
        for half in range(2):
            bk = half
            prog.wait("tensor", cv, bank_free[bk])
            for hp in range(NCH):
                prog.on("tensor", lambda e, qt=qt, hp=hp, half=half, bk=bk: e.matmul(
                    pscore[bk][:], ctxn[:, hp, qt * 128:(qt + 1) * 128],
                    wo_sb[:, hp, half * 512:(half + 1) * 512],
                    start=(hp == 0), stop=(hp == NCH - 1)),
                    mm if hp == NCH - 1 else None)
            prog.wait("vector", mm)
            prog.on("vector", lambda e, half=half, bk=bk: e.tensor_copy(
                s0st[:, half * 512:(half + 1) * 512], pscore[bk][:]), cv)
            bank_free[bk] = cv.n
        prog.wait("sync", cv)
        dma(prog, "sync", s0_out[qt * 128:(qt + 1) * 128, :], s0st[:], st)
        prog.wait("vector", st)
    prog.wait("sync", st)
    prog.wait("gpsimd", st)

    with nc.Block() as block:
        prog.finalize(block)
    ctx.close()
    return nc



def _prep_common(inputs):
    c = {}
    c["ident"] = np.eye(128, dtype=np.float32)
    return c



NW0 = SPAN0 // WIN


def host_rotary(h):
    half = np.arange(0, DH, 2, dtype=np.float64)
    inv_freq = 1.0 / 10000.0 ** (half / DH)
    t = np.arange(S, dtype=np.float64)
    freqs = t[:, None] * inv_freq[None, :]
    freqs = np.concatenate([freqs, freqs], -1)
    base_scale = (half + 0.4 * DH) / (1.4 * DH)
    power = (t - S // 2) / (WIN // 2)
    scale = base_scale[None, :] ** power[:, None]
    scale = np.concatenate([scale, scale], -1)
    cos, sin = np.cos(freqs), np.sin(freqs)
    sigma = np.where(np.arange(DH) < 32, -1.0, 1.0)[None, :]
    tabs = [cos * scale, sin * scale * sigma, cos / scale, sin / scale * sigma]
    ts = np.clip(np.arange(QS0[h] - 1, QS0[h] + SPAN0), 0, S - 1)
    out = np.stack([np.concatenate([tb[ts]] * 2, axis=1).T for tb in tabs])
    return np.ascontiguousarray(out.astype(np.float32))


def host_lmask():
    m = np.zeros((2, 2, 128, 512), np.float32)
    for first in range(2):
        for qs in range(2):
            r = qs * 128 + np.arange(128)[:, None]
            col = np.arange(512)[None, :]
            if first:
                allowed = (col >= 256) & (col <= 256 + r)
            else:
                allowed = (col >= r) & (col <= r + 256)
            m[first, qs] = np.where(allowed, 0.0, BIG)
    return m


def host_selbc(bc_scale=1.0):
    sel = np.zeros((NCH, 128, 16), np.float32)
    bc = np.zeros((NCH, 16, 128), np.float32)
    for ch in range(NCH):
        for half in range(2):
            sel[ch, half * 64:(half + 1) * 64, 2 * ch + half] = 1.0
            bc[ch, 2 * ch + half, half * 64:(half + 1) * 64] = bc_scale
    return sel, bc


def host_perm():
    P = np.zeros((128, 128), np.float32)
    for i in range(128):
        j = i % 64
        P[i - j + (j + 32) % 64, i] = 1.0
    return P


def build_l0ffn_layer(h):
    nc = bass.Bass("TRN2", target_bir_lowering=False)
    ctx = ExitStack()

    xin = nc.dram_tensor("xin", [128, NCH, SPAN0 + 1], F32, kind="ExternalInput").ap()
    wqkv = nc.dram_tensor("wqkv", [D, 3 * D], F32, kind="ExternalInput").ap()
    wo_in = nc.dram_tensor("wo_in", [D, D], F32, kind="ExternalInput").ap()
    w1p_in = nc.dram_tensor("w1p_in", [D, 22, 2, 128], F32, kind="ExternalInput").ap()
    b1a_in = nc.dram_tensor("b1a_in", [128, 22], F32, kind="ExternalInput").ap()
    b1g_in = nc.dram_tensor("b1g_in", [128, 22], F32, kind="ExternalInput").ap()
    w2p_in = nc.dram_tensor("w2p_in", [2816, D], F32, kind="ExternalInput").ap()
    b2_in = nc.dram_tensor("b2_in", [128, NCH], F32, kind="ExternalInput").ap()
    qs_in = nc.dram_tensor("qs_in", [128, 1], F32, kind="ExternalInput").ap()
    ks_in = nc.dram_tensor("ks_in", [128, 1], F32, kind="ExternalInput").ap()
    lng_in = nc.dram_tensor("lng_in", [128, NCH], F32, kind="ExternalInput").ap()
    lnb_in = nc.dram_tensor("lnb_in", [128, NCH], F32, kind="ExternalInput").ap()
    tabs_in = nc.dram_tensor("tabs_in", [128, 4, SPAN0 + 1], F32, kind="ExternalInput").ap()
    mask_in = nc.dram_tensor("mask_in", [128, 2, 2, 512], F32, kind="ExternalInput").ap()
    ident_in = nc.dram_tensor("ident_in", [128, 128], F32, kind="ExternalInput").ap()
    perm_in = nc.dram_tensor("perm_in", [128, 128], F32, kind="ExternalInput").ap()
    sel_in = nc.dram_tensor("sel_in", [128, NCH, 16], F32, kind="ExternalInput").ap()
    bc_in = nc.dram_tensor("bc_in", [16, NCH, 128], F32, kind="ExternalInput").ap()
    xout = nc.dram_tensor("xout", [128, NCH, SPAN0], F32, kind="ExternalOutput").ap()
    xmid = nc.dram_tensor("xmid", [128, NCH, SPAN0], F32,
                          kind="ExternalOutput" if DEBUG_B else "Internal").ap()
    if DEBUG_B:
        d_ctxw = nc.dram_tensor("d_ctxw", [128, NCH, WIN], F32, kind="ExternalOutput").ap()
        d_qp = nc.dram_tensor("d_qp", [128, NCH, WIN], F32, kind="ExternalOutput").ap()
        d_kr = nc.dram_tensor("d_kr", [128, NCH, WIN], F32, kind="ExternalOutput").ap()
        d_vr = nc.dram_tensor("d_vr", [128, 2, D], F32, kind="ExternalOutput").ap()
        d_P = nc.dram_tensor("d_P", [128, 512], F32, kind="ExternalOutput").ap()
        d_xn = nc.dram_tensor("d_xn", [128, NCH, SPAN0], F32, kind="ExternalOutput").ap()
        d_prod = nc.dram_tensor("d_prod", [128, 22, SPAN0 // 2], F32, kind="ExternalOutput").ap()
        d_sneg = nc.dram_tensor("d_sneg", [128, 512], F32, kind="ExternalOutput").ap()
        d_nm = nc.dram_tensor("d_nm", [128, 4], F32, kind="ExternalOutput").ap()
        d_Pn = nc.dram_tensor("d_Pn", [128, 512], F32, kind="ExternalOutput").ap()

    sb = lambda n, shp, dt: ctx.enter_context(nc.sbuf_tensor(n, shp, dt))
    psum = lambda n, shp, dt: ctx.enter_context(nc.psum_tensor(n, shp, dt))

    xbf = sb("xbf", [128, NCH, SPAN0 + 1], BF16)
    ident = sb("ident", [128, 128], BF16)
    perm = sb("perm", [128, 128], BF16)
    onesb = sb("onesb", [128, 1], BF16)
    epsb = sb("epsb", [128, 1], F32)
    qsb = sb("qsb", [128, 1], F32)
    ksb = sb("ksb", [128, 1], F32)
    lng = sb("lng", [128, NCH], F32)
    lnb = sb("lnb", [128, NCH], F32)
    b2s = sb("b2s", [128, NCH], F32)
    b1a = sb("b1a", [128, 22], F32)
    b1g = sb("b1g", [128, 22], F32)
    stage = sb("stage", [128, NCH, WIN], F32)
    stage2 = sb("stage2", [128, NCH, WIN], F32)
    nm1 = sb("nm1", [128, 1], F32)
    nm2 = sb("nm2", [128, 1], F32)
    sums = sb("sums", [128, 1], F32)
    sums2 = sb("sums2", [128, 1], F32)
    rsum = sb("rsum", [128, 1], F32)

    pbig = [psum(f"pbig{i}", [128, 512], F32) for i in range(4)]
    ptr = [psum(f"ptr{i}", [128, 1024], BF16) for i in range(2)]
    pctx = psum("pctx", [128, 256], F32)
    paux = psum("paux", [128, 512], F32)

    prog = Prog(nc, ctx)
    ldw = prog.sem("ldw")
    ldx = prog.sem("ldx")
    mm = prog.sem("mm")
    cv = prog.sem("cv")
    ac = prog.sem("ac")
    pl = prog.sem("pl")
    st = prog.sem("st")

    dma(prog, "gpsimd", ident[:], ident_in, ldw)
    dma(prog, "gpsimd", perm[:], perm_in, ldw)
    dma(prog, "gpsimd", xbf[:], xin, ldw)
    prog.on("gpsimd", lambda e: e.memset(epsb[:], 1e-5), pl)
    prog.on("gpsimd", lambda e: e.memset(onesb[:], 1.0), pl)
    dma(prog, "sync", qsb[:], qs_in, ldx)
    dma(prog, "sync", ksb[:], ks_in, ldx)
    dma(prog, "sync", lng[:], lng_in, ldx)
    dma(prog, "sync", lnb[:], lnb_in, ldx)
    dma(prog, "sync", b2s[:], b2_in, ldx)
    dma(prog, "sync", b1a[:], b1a_in, ldx)
    dma(prog, "sync", b1g[:], b1g_in, ldx)

    actx = ExitStack()
    sba = lambda n, shp, dt: actx.enter_context(nc.sbuf_tensor(n, shp, dt))
    masks = sba("masks", [128, 2, 2, 512], BF16)
    sel16 = sba("sel16", [128, NCH, 16], BF16)
    bc16 = sba("bc16", [16, NCH, 128], BF16)
    QA = sba("QA", [128, SPAN0 + 1], BF16)
    QB = sba("QB", [128, SPAN0 + 1], BF16)
    KA = sba("KA", [128, SPAN0 + 1], BF16)
    KB = sba("KB", [128, SPAN0 + 1], BF16)
    wqkv_sb = sba("wqkv_sb", [128, NCH, 3 * D], BF16)
    wo_sb = sba("wo_sb", [128, NCH, D], BF16)
    qp = sba("qp", [128, 2, NCH, WIN], BF16)
    kring = sba("kring", [128, NCH, 3, WIN], BF16)
    vring = sba("vring", [128, 6, D], BF16)
    qraw = sba("qraw", [128, NCH, WIN], BF16)
    q2b = sba("q2b", [128, NCH, WIN], BF16)
    rnb = sba("rnb", [16, WIN], F32)
    rnb2 = sba("rnb2", [16, WIN], F32)
    rnb16 = sba("rnb16", [16, WIN], BF16)
    u1 = sba("u1", [128, WIN], BF16)
    u2 = sba("u2", [128, WIN], BF16)
    sneg = sba("sneg", [128, 512], F32)
    Pw = sba("Pw", [128, 512], BF16)
    PT = sba("PT", [128, 4, 128], BF16)
    ctxw = sba("ctxw", [128, NCH, WIN], BF16)

    dma(prog, "gpsimd", masks[:], mask_in, ldw)
    dma(prog, "gpsimd", sel16[:], sel_in, ldw)
    dma(prog, "gpsimd", bc16[:], bc_in, ldw)
    for _i, _T in enumerate(None for _ in ()):
        pass
    dma(prog, "gpsimd", QA[:], tabs_in[:, 0, :], ldw)
    dma(prog, "gpsimd", QB[:], tabs_in[:, 1, :], ldw)
    dma(prog, "gpsimd", KA[:], tabs_in[:, 2, :], ldw)
    dma(prog, "gpsimd", KB[:], tabs_in[:, 3, :], ldw)
    dma(prog, "gpsimd", wqkv_sb[:], wqkv.rearrange("(c p) m -> p c m", p=128), ldw)
    dma(prog, "gpsimd", wo_sb[:], wo_in.rearrange("(c p) m -> p c m", p=128), ldw)
    prog.on("gpsimd", lambda e: e.memset(kring[:, :, 2, :], 0.0), pl)
    prog.on("gpsimd", lambda e: e.memset(vring[:, 4:6, :], 0.0), pl)

    prog.wait("vector", ldx)
    prog.wait("vector", ldw)
    prog.wait("tensor", ldw)
    prog.wait("tensor", pl)
    prog.wait("scalar", pl)
    prog.wait("gpsimd", pl)

    bank_free = [0, 0, 0, 0]
    ptr_free = [0, 0]
    pctx_free = [0]
    paux_free = [0]

    def emit_qkv_proj(w):
        for part in range(2):
            ocbase = part * NCH
            for oc in range(NCH):
                bk = oc % 2
                prog.wait("tensor", cv, bank_free[bk])
                for kc in range(NCH):
                    off = 1 if kc < 4 else 0
                    prog.on("tensor", lambda e, oc=oc, kc=kc, off=off, bk=bk, ob=ocbase, w=w: e.matmul(
                        pbig[bk][:, 0:WIN],
                        wqkv_sb[:, kc, (ob + oc) * 128:(ob + oc + 1) * 128],
                        xbf[:, kc, w * WIN + off: w * WIN + off + WIN],
                        start=(kc == 0), stop=(kc == NCH - 1)),
                        mm if kc == NCH - 1 else None)
                prog.wait("vector", mm)
                prog.on("vector", lambda e, oc=oc, bk=bk: e.tensor_copy(
                    qraw[:, oc, :], pbig[bk][:, 0:WIN]), cv)
                bank_free[bk] = cv.n
            prog.wait("scalar", cv)
            prog.on("scalar", lambda e: e.activation(out=q2b[:], in_=qraw[:],
                                                     func=AF.Square), ac)
            prog.wait("tensor", ac)
            prog.wait("tensor", cv, paux_free[0])
            for oc in range(NCH):
                prog.on("tensor", lambda e, oc=oc: e.matmul(
                    paux[0:16, 0:WIN], sel16[:, oc, :], q2b[:, oc, :],
                    start=(oc == 0), stop=(oc == NCH - 1)),
                    mm if oc == NCH - 1 else None)
            prog.wait("scalar", mm)
            prog.on("scalar", lambda e: e.activation(out=rnb[:], in_=paux[0:16, 0:WIN],
                                                     func=AF.Sqrt), ac)
            prog.wait("vector", ac)
            prog.on("vector", lambda e: e.tensor_scalar(
                out=rnb2[:], in0=rnb[:], scalar1=1e-12, scalar2=None,
                op0=OP.max), cv)
            prog.wait("vector", cv)
            prog.on("vector", lambda e: e.reciprocal(rnb2[:], rnb2[:]), cv)
            paux_free[0] = cv.n
            prog.wait("vector", cv)
            prog.on("vector", lambda e: e.tensor_copy(rnb16[:], rnb2[:]), cv)
            prog.wait("vector", cv)
            AT, BT = (QA, QB) if part == 0 else (KA, KB)
            for oc in range(NCH):
                bk = 2 + oc % 2
                prog.wait("tensor", cv, bank_free[bk])
                prog.on("tensor", lambda e, oc=oc, bk=bk: e.matmul(
                    pbig[bk][:, 0:WIN], perm[:], qraw[:, oc, :],
                    start=True, stop=True), mm)
                prog.wait("tensor", cv, pctx_free[0])
                prog.on("tensor", lambda e, oc=oc: e.matmul(
                    pctx[:, 0:WIN], bc16[:, oc, :], rnb16[:],
                    start=True, stop=True), mm)
                prog.wait("vector", mm)
                cw = slice(w * WIN + 1, w * WIN + 1 + WIN)
                prog.on("vector", lambda e, oc=oc, cw=cw, AT=AT: e.tensor_tensor(
                    out=u1[:], in0=qraw[:, oc, :], in1=AT[:, cw], op=OP.mult), cv)
                prog.on("vector", lambda e, oc=oc, bk=bk, cw=cw, BT=BT: e.tensor_tensor(
                    out=u2[:], in0=pbig[bk][:, 0:WIN], in1=BT[:, cw], op=OP.mult), cv)
                bank_free[bk] = cv.n
                prog.on("vector", lambda e: e.tensor_tensor(
                    out=u1[:], in0=u1[:], in1=u2[:], op=OP.add), cv)
                prog.wait("vector", cv)
                if part == 0:
                    dst, scl = qp[:, w % 2, oc, :], qsb
                else:
                    dst, scl = kring[:, oc, w % 3, :], ksb
                prog.on("vector", lambda e, dst=dst, scl=scl: e.scalar_tensor_tensor(
                    out=dst, in0=u1[:], scalar=scl[:], in1=pctx[:, 0:WIN],
                    op0=OP.mult, op1=OP.mult), cv)
                pctx_free[0] = cv.n
        for tt in range(2):
            gt = w * 2 + tt
            for half in range(2):
                bk = half
                prog.wait("tensor", cv, bank_free[bk])
                for kc in range(NCH):
                    off = 1 if kc < 4 else 0
                    prog.on("tensor", lambda e, tt=tt, kc=kc, half=half, off=off, bk=bk, w=w: e.matmul(
                        pbig[bk][:],
                        xbf[:, kc, w * WIN + off + tt * 128: w * WIN + off + tt * 128 + 128],
                        wqkv_sb[:, kc, 2 * D + half * 512: 2 * D + (half + 1) * 512],
                        start=(kc == 0), stop=(kc == NCH - 1)),
                        mm if kc == NCH - 1 else None)
                prog.wait("vector", mm)
                prog.on("vector", lambda e, gt=gt, half=half, bk=bk: e.tensor_copy(
                    vring[:, gt % 6, half * 512:(half + 1) * 512], pbig[bk][:]), cv)
                bank_free[bk] = cv.n

    def emit_attn_window(w):
        first = 1 if w == 0 else 0
        prog.wait("tensor", cv)
        for qs in range(2):
            for c in range(NCH):
                for hh in range(2):
                    b0 = hh * 64
                    bk = (qs * 2 + hh) % 4
                    prog.wait("tensor", cv, bank_free[bk])
                    prog.on("tensor", lambda e, qs=qs, c=c, hh=hh, b0=b0, bk=bk, w=w: e.matmul(
                        pbig[bk][:, 0:256],
                        qp[b0:b0 + 64, w % 2, c, qs * 128:(qs + 1) * 128],
                        kring[b0:b0 + 64, c, (w - 1) % 3, :],
                        start=True, stop=True), mm)
                    prog.on("tensor", lambda e, qs=qs, c=c, hh=hh, b0=b0, bk=bk, w=w: e.matmul(
                        pbig[bk][:, 256:512],
                        qp[b0:b0 + 64, w % 2, c, qs * 128:(qs + 1) * 128],
                        kring[b0:b0 + 64, c, w % 3, :],
                        start=True, stop=True), mm)
                    prog.wait("vector", mm)
                    prog.wait("vector", ac)
                    prog.wait("vector", pl)
                    prog.on("vector", lambda e, first=first, qs=qs, bk=bk: e.tensor_tensor(
                        out=sneg[:], in0=masks[:, first, qs, :], in1=pbig[bk][:],
                        op=OP.subtract), cv)
                    bank_free[bk] = cv.n
                    prog.on("vector", lambda e: e.tensor_reduce(
                        out=nm1[:], in_=sneg[:], axis=AX.X, op=OP.min), cv)
                    prog.wait("scalar", cv)
                    prog.on("scalar", lambda e: e.activation(
                        out=Pw[:], in_=sneg[:], func=AF.Exp, bias=nm1[:],
                        scale=-1.0), ac)
                    prog.wait("vector", ac)
                    prog.on("vector", lambda e: e.tensor_reduce(
                        out=sums[:], in_=Pw[:], axis=AX.X, op=OP.add), cv)
                    prog.wait("vector", cv)
                    prog.on("vector", lambda e: e.reciprocal(rsum[:], sums[:]), cv)
                    prog.wait("gpsimd", cv)
                    prog.on("gpsimd", lambda e: e.tensor_scalar(
                        out=Pw[:], in0=Pw[:], scalar1=rsum[:], scalar2=None,
                        op0=OP.mult), pl)
                    if DEBUG_B and w == DEBUG_BW and qs == 1 and c == 0 and hh == 0:
                        prog.wait("gpsimd", pl)
                        dma(prog, "gpsimd", d_sneg, sneg[:], st)
        
                        prog.wait("vector", pl)
                        prog.on("vector", lambda e: e.tensor_copy(stage[:, 0, 0:1], nm1[:]), cv)
                        prog.on("vector", lambda e: e.tensor_copy(stage[:, 0, 1:2], nm2[:]), cv)
                        prog.on("vector", lambda e: e.tensor_copy(stage[:, 0, 2:3], sums[:]), cv)
                        prog.on("vector", lambda e: e.tensor_copy(stage[:, 0, 3:4], rsum[:]), cv)
                        prog.wait("gpsimd", cv)
                        dma(prog, "gpsimd", d_nm, stage[:, 0, 0:4], st)
                        dma(prog, "gpsimd", d_Pn, Pw[:], st)
                        prog.wait("vector", st)
                        prog.wait("scalar", st)
                        prog.wait("tensor", st)
                    pb = (qs * 2 + hh) % 2
                    prog.wait("tensor", pl)
                    prog.wait("tensor", cv, ptr_free[pb])
                    for t in range(4):
                        prog.on("tensor", lambda e, t=t, pb=pb: e.transpose(
                            ptr[pb][:, t * 128:(t + 1) * 128],
                            Pw[:, t * 128:(t + 1) * 128], ident[:]),
                            mm if t == 3 else None)
                    prog.wait("vector", mm)
                    prog.on("vector", lambda e, pb=pb: e.tensor_copy(
                        PT[:, :, :],
                        ptr[pb].ap()[:, 0:512].rearrange("p (c f) -> p c f", c=4)), cv)
                    ptr_free[pb] = cv.n
                    prog.wait("tensor", cv)
                    prog.wait("tensor", cv, pctx_free[0])
                    for t in range(4):
                        gt = (w - 1) * 2 + t
                        prog.on("tensor", lambda e, t=t, gt=gt, c=c, hh=hh, b0=b0: e.matmul(
                            pctx[b0:b0 + 64, 0:128],
                            vring[:, gt % 6, (2 * c + hh) * 64:(2 * c + hh + 1) * 64],
                            PT[:, t, :], start=(t == 0), stop=(t == 3)),
                            mm if t == 3 else None)
                prog.wait("vector", mm)
                prog.on("vector", lambda e, c=c, qs=qs: e.tensor_copy(
                    ctxw[:, c, qs * 128:(qs + 1) * 128], pctx[:, 0:128]), cv)
                pctx_free[0] = cv.n

    def emit_wo_residual(w):
        prog.wait("sync", cv)
        dma(prog, "sync", stage[:], xin[:, :, w * WIN + 1: w * WIN + 1 + WIN], ldx)
        prog.wait("tensor", cv)
        for oc in range(NCH):
            bk = oc % 2
            prog.wait("tensor", cv, bank_free[bk])
            for hp in range(NCH):
                prog.on("tensor", lambda e, oc=oc, hp=hp, bk=bk: e.matmul(
                    pbig[bk][:, 0:WIN], wo_sb[:, hp, oc * 128:(oc + 1) * 128],
                    ctxw[:, hp, :], start=(hp == 0), stop=(hp == NCH - 1)),
                    mm if hp == NCH - 1 else None)
            prog.wait("vector", mm)
            prog.wait("vector", ldx)
            prog.on("vector", lambda e, oc=oc, bk=bk: e.tensor_tensor(
                out=stage2[:, oc, :], in0=pbig[bk][:, 0:WIN], in1=stage[:, oc, :],
                op=OP.add), cv)
            bank_free[bk] = cv.n
        prog.wait("sync", cv)
        dma(prog, "sync", xmid[:, :, w * WIN: w * WIN + WIN], stage2[:], st)
        prog.wait("vector", st)

    for w in range(NW0):
        emit_qkv_proj(w)
        if DEBUG_B and w == DEBUG_BW:
            prog.wait("gpsimd", cv)
            dma(prog, "gpsimd", d_qp, qp[:, w % 2, :, :], st)
            dma(prog, "gpsimd", d_kr, kring[:, :, w % 3, :], st)
            dma(prog, "gpsimd", d_vr, vring[:, (2 * DEBUG_BW) % 6:(2 * DEBUG_BW) % 6 + 2, :], st)
        emit_attn_window(w)
        if DEBUG_B and w == DEBUG_BW:
            prog.wait("gpsimd", cv)
            prog.wait("gpsimd", pl)
            dma(prog, "gpsimd", d_ctxw, ctxw[:], st)
            dma(prog, "gpsimd", d_P, Pw[:], st)
            prog.wait("vector", st)
            prog.wait("scalar", st)
        emit_wo_residual(w)

    attn_done_cv = cv.n
    attn_done_st = st.n
    prog.wait("gpsimd", cv, attn_done_cv)
    prog.wait("gpsimd", st, attn_done_st)
    prog.wait("sync", cv, attn_done_cv)
    dma(prog, "gpsimd", xbf[:, :, 1:SPAN0 + 1], xmid, ldw)
    prog.wait("tensor", ldw)
    prog.wait("scalar", ldw)
    prog.wait("vector", ldw)
    actx.close()

    fctx = ExitStack()
    sbf = lambda n, shp, dt: fctx.enter_context(nc.sbuf_tensor(n, shp, dt))
    xn_full = sbf("xn_full", [128, NCH, SPAN0], BF16)
    x2w = sbf("x2w", [128, NCH, WIN], BF16)
    mrow = sbf("mrow", [1, WIN], F32)
    srow = sbf("srow", [1, WIN], F32)
    vrow = sbf("vrow", [1, WIN], F32)
    sdrow = sbf("sdrow", [1, WIN], F32)
    rstdrow = sbf("rstdrow", [1, WIN], F32)
    mrrow = sbf("mrrow", [1, WIN], F32)
    rstdb16 = sbf("rstdb16", [1, WIN], BF16)
    mrb16 = sbf("mrb16", [1, WIN], BF16)
    rbc = sbf("rbc", [128, WIN], BF16)
    mbc = sbf("mbc", [128, WIN], BF16)
    u1f = sbf("u1f", [128, WIN], BF16)
    w1c = sbf("w1c", [128, 2, NCH, 2, 128], BF16)
    w2c = sbf("w2c", [128, 2, 22, 128], BF16)
    glw = sbf("glw", [128, 512], BF16)
    prod = sbf("prod", [128, 22, SPAN0 // 2], BF16)
    onesrow = sbf("onesrow", [1, 128], BF16)
    fstage = sbf("fstage", [128, 512], F32)
    fstage2 = sbf("fstage2", [128, 512], F32)

    prog.on("gpsimd", lambda e: e.memset(onesrow[:], 1.0), pl)
    prog.wait("tensor", pl)

    for w in range(NW0):
        prog.wait("tensor", cv, paux_free[0])
        for kc in range(NCH):
            off = 1 if kc < 4 else 0
            prog.on("tensor", lambda e, kc=kc, off=off, w=w: e.matmul(
                paux[0:1, 0:WIN], onesb[:, 0:1],
                xbf[:, kc, w * WIN + off: w * WIN + off + WIN],
                start=(kc == 0), stop=(kc == NCH - 1)),
                mm if kc == NCH - 1 else None)
        prog.wait("scalar", cv)
        prog.on("scalar", lambda e, w=w: e.activation(out=x2w[:, 0, :], in_=None,
            func=AF.Square) if False else None)
        prog.q["scalar"].pop()
        for kc in range(NCH):
            off = 1 if kc < 4 else 0
            prog.on("scalar", lambda e, kc=kc, off=off, w=w: e.activation(
                out=x2w[:, kc, :], in_=xbf[:, kc, w * WIN + off: w * WIN + off + WIN],
                func=AF.Square), ac if kc == NCH - 1 else None)
        prog.wait("tensor", ac)
        for kc in range(NCH):
            prog.on("tensor", lambda e, kc=kc: e.matmul(
                paux[0:1, 256:256 + WIN], onesb[:, 0:1], x2w[:, kc, :],
                start=(kc == 0), stop=(kc == NCH - 1)),
                mm if kc == NCH - 1 else None)
        prog.wait("vector", mm)
        prog.on("vector", lambda e: e.tensor_scalar(
            out=mrow[:], in0=paux[0:1, 0:WIN], scalar1=1.0 / D, scalar2=None,
            op0=OP.mult), cv)
        prog.on("vector", lambda e: e.tensor_scalar(
            out=srow[:], in0=paux[0:1, 256:256 + WIN], scalar1=1.0 / D, scalar2=None,
            op0=OP.mult), cv)
        paux_free[0] = cv.n
        prog.wait("vector", cv)
        prog.on("vector", lambda e: e.tensor_tensor(
            out=vrow[:], in0=mrow[:], in1=mrow[:], op=OP.mult), cv)
        prog.wait("vector", cv)
        prog.on("vector", lambda e: e.tensor_tensor(
            out=vrow[:], in0=srow[:], in1=vrow[:], op=OP.subtract), cv)
        prog.wait("scalar", cv)
        prog.on("scalar", lambda e: e.activation(
            out=sdrow[:], in_=vrow[:], func=AF.Sqrt, bias=epsb[0:1, :],
            scale=1.0), ac)
        prog.wait("vector", ac)
        prog.on("vector", lambda e: e.reciprocal(rstdrow[:], sdrow[:]), cv)
        prog.wait("vector", cv)
        prog.on("vector", lambda e: e.tensor_tensor(
            out=mrrow[:], in0=mrow[:], in1=rstdrow[:], op=OP.mult), cv)
        prog.on("vector", lambda e: e.tensor_copy(rstdb16[:], rstdrow[:]), cv)
        prog.wait("vector", cv)
        prog.on("vector", lambda e: e.tensor_copy(mrb16[:], mrrow[:]), cv)
        prog.wait("tensor", cv)
        prog.wait("tensor", cv, paux_free[0])
        prog.on("tensor", lambda e: e.matmul(
            paux[:, 0:WIN], onesrow[:], rstdb16[:], start=True, stop=True), mm)
        prog.on("tensor", lambda e: e.matmul(
            paux[:, 256:256 + WIN], onesrow[:], mrb16[:], start=True, stop=True), mm)
        prog.wait("vector", mm)
        prog.on("vector", lambda e: e.tensor_copy(rbc[:], paux[:, 0:WIN]), cv)
        prog.on("vector", lambda e: e.tensor_copy(mbc[:], paux[:, 256:256 + WIN]), cv)
        paux_free[0] = cv.n
        prog.wait("vector", cv)
        for kc in range(NCH):
            off = 1 if kc < 4 else 0
            prog.on("vector", lambda e, kc=kc, off=off, w=w: e.tensor_tensor(
                out=u1f[:], in0=xbf[:, kc, w * WIN + off: w * WIN + off + WIN],
                in1=rbc[:], op=OP.mult), cv)
            prog.wait("vector", cv)
            prog.on("vector", lambda e: e.tensor_tensor(
                out=u1f[:], in0=u1f[:], in1=mbc[:], op=OP.subtract), cv)
            prog.wait("vector", cv)
            prog.on("vector", lambda e, kc=kc, w=w: e.tensor_scalar(
                out=xn_full[:, kc, w * WIN: w * WIN + WIN], in0=u1f[:],
                scalar1=lng[:, kc:kc + 1], scalar2=lnb[:, kc:kc + 1],
                op0=OP.mult, op1=OP.add), cv)

    xn_done = cv.n
    if DEBUG_B:
        prog.wait("gpsimd", cv)
        dma(prog, "gpsimd", d_xn, xn_full[:], st)

    NSC = 2
    SCW = SPAN0 // NSC
    w1_free = [0, 0]
    w2_free = [0, 0]
    for sc in range(NSC):
        ncols = [512, 512, 128]
        for pair in range(22):
            wb = pair % 2
            prog.wait("gpsimd", mm, w1_free[wb])
            dma(prog, "gpsimd", w1c[:, wb, :, :, :],
                w1p_in.rearrange("(c p) a b m -> p c a b m", p=128)[:, :, pair, :, :],
                ldw)
            prog.wait("tensor", ldw)
            col0 = 0
            for ncol in ncols:
                cols = slice(sc * SCW + col0, sc * SCW + col0 + ncol)
                pcols = slice(col0, col0 + ncol)
                for half in range(2):
                    bk = half
                    prog.wait("tensor", cv, bank_free[bk])
                    prog.wait("tensor", ac)
                    for kc in range(NCH):
                        prog.on("tensor", lambda e, kc=kc, half=half, bk=bk, cols=cols, ncol=ncol, wb=wb: e.matmul(
                            pbig[bk][:, 0:ncol], w1c[:, wb, kc, half, :],
                            xn_full[:, kc, cols],
                            start=(kc == 0), stop=(kc == NCH - 1)),
                            mm if kc == NCH - 1 else None)
                prog.wait("scalar", mm)
                prog.on("scalar", lambda e, pair=pair, ncol=ncol: e.activation(
                    out=glw[:, 0:ncol], in_=pbig[1][:, 0:ncol], func=AF.Gelu,
                    bias=b1g[:, pair:pair + 1], scale=1.0), ac)
                prog.wait("vector", ac)
                prog.on("vector", lambda e, pair=pair, ncol=ncol, cols=cols, col0=col0, sc=sc: e.scalar_tensor_tensor(
                    out=prod[:, pair, col0:col0 + ncol], in0=pbig[0][:, 0:ncol],
                    scalar=b1a[:, pair:pair + 1], in1=glw[:, 0:ncol],
                    op0=OP.add, op1=OP.mult), cv)
                bank_free[0] = cv.n
                bank_free[1] = cv.n
                col0 += ncol
            w1_free[wb] = mm.n
        for oc in range(NCH):
            wb = oc % 2
            prog.wait("gpsimd", mm, w2_free[wb])
            dma(prog, "gpsimd", w2c[:, wb, :, :],
                w2p_in.rearrange("(c p) m -> p c m", p=128)[:, :, oc * 128:(oc + 1) * 128],
                ldw)
            prog.wait("tensor", ldw)
            col0 = 0
            for ncol in ncols:
                cols = slice(sc * SCW + col0, sc * SCW + col0 + ncol)
                bk = 2 + (col0 // 512) % 2
                prog.wait("tensor", cv, bank_free[bk])
                for f in range(22):
                    prog.on("tensor", lambda e, f=f, oc=oc, bk=bk, col0=col0, ncol=ncol, wb=wb: e.matmul(
                        pbig[bk][:, 0:ncol], w2c[:, wb, f, :],
                        prod[:, f, col0:col0 + ncol],
                        start=(f == 0), stop=(f == 21)),
                        mm if f == 21 else None)
                dma(prog, "sync", fstage[:, 0:ncol], xmid[:, oc, cols], ldx)
                prog.wait("vector", mm)
                prog.wait("vector", ldx)
                prog.on("vector", lambda e, oc=oc, bk=bk, ncol=ncol: e.scalar_tensor_tensor(
                    out=fstage2[:, 0:ncol], in0=pbig[bk][:, 0:ncol],
                    scalar=b2s[:, oc:oc + 1], in1=fstage[:, 0:ncol],
                    op0=OP.add, op1=OP.add), cv)
                bank_free[bk] = cv.n
                prog.wait("sync", cv)
                dma(prog, "sync", xout[:, oc, cols], fstage2[:, 0:ncol], st)
                prog.wait("vector", st)
                col0 += ncol
            w2_free[wb] = mm.n

    if DEBUG_B:
        prog.wait("gpsimd", cv)
        dma(prog, "gpsimd", d_prod, prod[:], st)
    prog.wait("sync", st)
    prog.wait("gpsimd", st)

    with nc.Block() as block:
        prog.finalize(block)
    fctx.close()
    ctx.close()
    return nc


def build_combine():
    nc = bass.Bass("TRN2", target_bir_lowering=False)
    ctx = ExitStack()
    x0sp = nc.dram_tensor("x0sp", [SPAN0, D], F32, kind="ExternalInput").ap()
    s0sp = nc.dram_tensor("s0sp", [SPAN0, D], F32, kind="ExternalInput").ap()
    g0fm = nc.dram_tensor("g0fm", [128, NCH, SPAN0], F32, kind="ExternalInput").ap()
    ident_in = nc.dram_tensor("ident_in", [128, 128], F32, kind="ExternalInput").ap()
    x1sp = nc.dram_tensor("x1sp", [SPAN0, D], F32, kind="ExternalOutput").ap()
    sb = lambda n, shp, dt: ctx.enter_context(nc.sbuf_tensor(n, shp, dt))
    g_fm = sb("g_fm", [128, NCH, SPAN0], BF16)
    ident = sb("ident", [128, 128], BF16)
    gt = sb("gt", [128, D], BF16)
    x0t = sb("x0t", [128, 2, D], F32)
    s0t = sb("s0t", [128, 2, D], F32)
    x1t = sb("x1t", [128, D], F32)
    ptr = ctx.enter_context(nc.psum_tensor("ptr", [128, 1024], BF16))
    prog = Prog(nc, ctx)
    ldw = prog.sem("ldw"); ldx = prog.sem("ldx"); mm = prog.sem("mm")
    cv = prog.sem("cv"); ac = prog.sem("ac"); st = prog.sem("st")
    dma(prog, "gpsimd", ident[:], ident_in, ldw)
    dma(prog, "gpsimd", g_fm[:], g0fm, ldw)
    prog.wait("scalar", ldw)
    prog.on("scalar", lambda e: e.activation(
        out=g_fm.ap().rearrange("p c t -> p (c t)"),
        in_=g_fm.ap().rearrange("p c t -> p (c t)"), func=AF.Silu), ac)
    prog.wait("tensor", ac)
    prog.wait("tensor", ldw, 16)
    buf_free = [0, 0]
    for rt in range(NT0):
        buf = rt % 2
        prog.wait("sync", cv, buf_free[buf])
        dma(prog, "sync", x0t[:, buf, :], x0sp[rt * 128:(rt + 1) * 128, :], ldx)
        dma(prog, "sync", s0t[:, buf, :], s0sp[rt * 128:(rt + 1) * 128, :], ldx)
        for ch in range(NCH):
            prog.on("tensor", lambda e, c=ch, rt=rt: e.transpose(
                ptr[:, c * 128:(c + 1) * 128], g_fm[:, c, rt * 128:(rt + 1) * 128],
                ident[:]), mm if ch == NCH - 1 else None)
        prog.wait("vector", mm)
        prog.wait("vector", ldx)
        prog.on("vector", lambda e: e.tensor_copy(
            gt.ap().rearrange("p (c f) -> p c f", c=NCH),
            ptr.ap().rearrange("p (c f) -> p c f", c=NCH)), cv)
        prog.wait("vector", cv)
        prog.on("vector", lambda e, b=buf: e.tensor_tensor(
            out=x1t[:], in0=s0t[:, b, :], in1=gt[:], op=OP.mult), cv)
        prog.wait("vector", cv)
        prog.on("vector", lambda e, b=buf: e.tensor_tensor(
            out=x1t[:], in0=x1t[:], in1=x0t[:, b, :], op=OP.add), cv)
        buf_free[buf] = cv.n
        prog.wait("sync", cv)
        dma(prog, "sync", x1sp[rt * 128:(rt + 1) * 128, :], x1t[:], st)
        prog.wait("vector", st)
        prog.wait("tensor", cv)
    prog.wait("sync", st)
    with nc.Block() as block:
        prog.finalize(block)
    ctx.close()
    return nc



def build_l1(qoff):
    nc = bass.Bass("TRN2", target_bir_lowering=False)
    ctx = ExitStack()
    x1f = nc.dram_tensor("x1f", [S, D], F32, kind="ExternalInput").ap()
    wqT = nc.dram_tensor("wqT", [D, D], F32, kind="ExternalInput").ap()
    wkT = nc.dram_tensor("wkT", [D, D], F32, kind="ExternalInput").ap()
    wvT = nc.dram_tensor("wvT", [D, D], F32, kind="ExternalInput").ap()
    bias1 = nc.dram_tensor("bias1", [16, 128, 6016], F32, kind="ExternalInput").ap()
    gam_in = nc.dram_tensor("gam_in", [1, D], F32, kind="ExternalInput").ap()
    bet_in = nc.dram_tensor("bet_in", [1, D], F32, kind="ExternalInput").ap()
    w1T = nc.dram_tensor("w1T", [D, 4096], F32, kind="ExternalInput").ap()
    b1_in = nc.dram_tensor("b1_in", [128, 32], F32, kind="ExternalInput").ap()
    w2T = nc.dram_tensor("w2T", [4096, D], F32, kind="ExternalInput").ap()
    b2_in = nc.dram_tensor("b2_in", [128, NCH], F32, kind="ExternalInput").ap()
    ident_in = nc.dram_tensor("ident_in", [128, 128], F32, kind="ExternalInput").ap()
    ones1_in = nc.dram_tensor("ones1_in", [1, 128], F32, kind="ExternalInput").ap()
    out_sp = nc.dram_tensor("out_sp", [SPAN1, D], F32, kind="ExternalOutput").ap()
    s1d = nc.dram_tensor("s1d", [SPAN1, D], F32, kind="Internal").ap()

    sb = lambda n, shp, dt: ctx.enter_context(nc.sbuf_tensor(n, shp, dt))
    psum = lambda n, shp, dt: ctx.enter_context(nc.psum_tensor(n, shp, dt))

    xn_fm = sb("xn_fm", [128, NCH, S], BF16)
    ident = sb("ident", [128, 128], BF16)
    onesr = sb("onesr", [1, 128], BF16)
    epsb = sb("epsb", [128, 1], F32)
    b2s = sb("b2s", [128, NCH], F32)
    b1s = sb("b1s", [128, 32], F32)
    gammab = sb("gammab", [128, D], BF16)
    betab = sb("betab", [128, D], BF16)
    pscore = [psum(f"pscore{i}", [128, 512], F32) for i in range(4)]
    ptr = [psum(f"ptr{i}", [128, 1024], BF16) for i in range(2)]
    pctx = psum("pctx", [128, 128], F32)
    pv2 = psum("pv2", [128, 512], F32)

    prog = Prog(nc, ctx)
    ldw = prog.sem("ldw"); ldx = prog.sem("ldx"); ldb = prog.sem("ldb")
    mm = prog.sem("mm"); cv = prog.sem("cv"); ac = prog.sem("ac")
    pl = prog.sem("pl"); st = prog.sem("st")

    dma(prog, "gpsimd", ident[:], ident_in, ldw)
    dma(prog, "gpsimd", onesr[:], ones1_in, ldw)
    prog.on("gpsimd", lambda e: e.memset(epsb[:], 1e-5), pl)
    dma(prog, "sync", b2s[:], b2_in, ldx)
    dma(prog, "sync", b1s[:], b1_in, ldx)
    prog.wait("scalar", pl)
    prog.wait("tensor", ldw, 32)

    ictx = ExitStack()
    sbi = lambda n, shp, dt: ictx.enter_context(nc.sbuf_tensor(n, shp, dt))
    xt = sbi("xt", [128, 2, D], F32)
    bns = sbi("bns", [128, 12], F32)
    mv = sbi("mv", [128, 2], F32)
    sd = sbi("sd", [128, 1], F32)
    rstd = sbi("rstd", [128, 1], F32)
    negmur = sbi("negmur", [128, 1], F32)
    xn_tm = sbi("xn_tm", [128, D], BF16)
    emit_ln_transpose(prog, x1f, S, xn_fm,
                      (xt, bns, mv, sd, rstd, negmur, xn_tm, ident, ptr[0], epsb),
                      (ldx, mm, cv, ac))
    phase1_cv = cv.n
    ictx.close()

    gctx = ExitStack()
    g16 = gctx.enter_context(nc.sbuf_tensor("g16", [1, D], BF16))
    b16 = gctx.enter_context(nc.sbuf_tensor("b16", [1, D], BF16))
    prog.wait("gpsimd", cv, phase1_cv)
    dma(prog, "gpsimd", g16[:], gam_in, ldb)
    dma(prog, "gpsimd", b16[:], bet_in, ldb)
    prog.wait("tensor", ldb)
    for half in range(2):
        prog.on("tensor", lambda e, half=half: e.matmul(
            pv2[:], onesr[:], g16[:, half * 512:(half + 1) * 512],
            start=True, stop=True), mm)
        prog.wait("vector", mm)
        prog.on("vector", lambda e, half=half: e.tensor_copy(
            gammab[:, half * 512:(half + 1) * 512], pv2[:]), cv)
        prog.wait("tensor", cv)
        prog.on("tensor", lambda e, half=half: e.matmul(
            pv2[:], onesr[:], b16[:, half * 512:(half + 1) * 512],
            start=True, stop=True), mm)
        prog.wait("vector", mm)
        prog.on("vector", lambda e, half=half: e.tensor_copy(
            betab[:, half * 512:(half + 1) * 512], pv2[:]), cv)
        prog.wait("tensor", cv)
    gb_cv = cv.n

    actx = ExitStack()
    sba = lambda n, shp, dt: actx.enter_context(nc.sbuf_tensor(n, shp, dt))
    w_hp = sba("w_hp", [128, 3, NCH, 128], BF16)
    Kb = sba("Kb", [128, S], BF16)
    Vtm = sba("Vtm", [128, S // 128, 128], BF16)
    Qb = sba("Qb", [128, SPAN1], BF16)
    B1b = sba("B1b", [128, 2, 6016], BF16)
    sneg = sba("sneg", [128, S], F32)
    scr = sba("scr", [128, 512], F32)
    nmstrip = sba("nmstrip", [128, 8], F32)
    nmstrip2 = sba("nmstrip2", [128, 8], F32)
    negmax = sba("negmax", [128, 1], F32)
    Pb = sba("Pb", [128, S], BF16)
    PT = sba("PT", [128, S // 128, 128], BF16)
    sums = sba("sums", [128, 1], F32)
    rsum = sba("rsum", [128, 1], F32)
    ctxn = sba("ctxn", [128, NCH, SPAN1], BF16)
    ott = sba("ott", [128, D], BF16)
    s1st = sba("s1st", [128, D], F32)

    ISC = float(1.0 / np.sqrt(np.sqrt(1024.0)))
    bank_free = [0, 0, 0, 0]
    ptr_free = [0, 0]
    pb_free = 0
    NKT = S // 512
    for hp in range(NH // 2):
        hA, hB = 2 * hp, 2 * hp + 1
        if hp > 0:
            prog.wait("gpsimd", mm)
            prog.wait("gpsimd", cv)
        for wi, w_ap in enumerate((wqT, wkT, wvT)):
            dma(prog, "gpsimd", w_hp[:, wi, :, :],
                w_ap.rearrange("(c p) m -> p c m", p=128)[:, :, hA * 64:(hB + 1) * 64],
                ldw)
        dma(prog, "gpsimd", B1b[:, 0, :], bias1[hA], ldb)
        dma(prog, "gpsimd", B1b[:, 1, :], bias1[hB], ldb)
        prog.wait("tensor", ldw)
        if hp == 0:
            prog.wait("tensor", cv, gb_cv)
        for kt in range(NKT):
            bk = kt % 2
            prog.wait("tensor", cv, bank_free[bk])
            for kc in range(NCH):
                prog.on("tensor", lambda e, kt=kt, kc=kc, bk=bk: e.matmul(
                    pscore[bk][:], w_hp[:, 1, kc, :],
                    xn_fm[:, kc, kt * 512:(kt + 1) * 512],
                    start=(kc == 0), stop=(kc == NCH - 1)),
                    mm if kc == NCH - 1 else None)
            prog.wait("vector", mm)
            prog.on("vector", lambda e, kt=kt, bk=bk: e.tensor_scalar(
                out=Kb[:, kt * 512:(kt + 1) * 512], in0=pscore[bk][:],
                scalar1=ISC, scalar2=None, op0=OP.mult), cv)
            bank_free[bk] = cv.n
        for q5 in range(SPAN1 // 512):
            bk = 2 + q5 % 2
            prog.wait("tensor", cv, bank_free[bk])
            for kc in range(NCH):
                prog.on("tensor", lambda e, q5=q5, kc=kc, bk=bk: e.matmul(
                    pscore[bk][:], w_hp[:, 0, kc, :],
                    xn_fm[:, kc, qoff + q5 * 512: qoff + (q5 + 1) * 512],
                    start=(kc == 0), stop=(kc == NCH - 1)),
                    mm if kc == NCH - 1 else None)
            prog.wait("vector", mm)
            prog.on("vector", lambda e, q5=q5, bk=bk: e.tensor_scalar(
                out=Qb[:, q5 * 512:(q5 + 1) * 512], in0=pscore[bk][:],
                scalar1=ISC, scalar2=None, op0=OP.mult), cv)
            bank_free[bk] = cv.n
        for tt in range(S // 128):
            bk = tt % 2
            prog.wait("tensor", cv, bank_free[bk])
            for kc in range(NCH):
                prog.on("tensor", lambda e, tt=tt, kc=kc, bk=bk: e.matmul(
                    pscore[bk][:, 0:128], xn_fm[:, kc, tt * 128:(tt + 1) * 128],
                    w_hp[:, 2, kc, :],
                    start=(kc == 0), stop=(kc == NCH - 1)),
                    mm if kc == NCH - 1 else None)
            prog.wait("vector", mm)
            prog.on("vector", lambda e, tt=tt, bk=bk: e.tensor_copy(
                Vtm[:, tt, :], pscore[bk][:, 0:128]), cv)
            bank_free[bk] = cv.n
        prog.wait("vector", ldb)

        for qt in range(NT1):
            for hh in range(2):
                base = hh * 64
                for kc in range(NKT):
                    bk = kc % 4
                    prog.wait("tensor", cv, bank_free[bk])
                    prog.on("tensor", lambda e, kc=kc, qt=qt, base=base, bk=bk: e.matmul(
                        pscore[bk][:], Qb[base:base + 64, qt * 128:(qt + 1) * 128],
                        Kb[base:base + 64, kc * 512:(kc + 1) * 512],
                        start=True, stop=True), mm)
                    prog.wait("vector", mm)
                    prog.on("vector", lambda e, kc=kc, bk=bk: e.tensor_scalar(
                        out=sneg[:, kc * 512:(kc + 1) * 512], in0=pscore[bk][:],
                        scalar1=-1.0, scalar2=3.0e38, op0=OP.mult, op1=OP.min,
                        accum_out=nmstrip[:, kc:kc + 1]), cv)
                    bank_free[bk] = cv.n
                prog.wait("scalar", cv)
                prog.on("scalar", lambda e: e.activation(
                    out=nmstrip2[:], in_=nmstrip[:], func=AF.Copy), ac)
                prog.wait("vector", ac)
                prog.on("vector", lambda e: e.tensor_reduce(
                    out=negmax[:], in_=nmstrip2[:], axis=AX.X, op=OP.min), cv)
                prog.wait("scalar", cv)
                prog.wait("scalar", mm, pb_free)
                prog.on("scalar", lambda e: e.activation(
                    out=Pb[:], in_=sneg[:], func=AF.Exp, bias=negmax[:],
                    scale=-1.0), ac)
                prog.wait("vector", ac)
                prog.on("vector", lambda e: e.tensor_reduce(
                    out=sums[:], in_=Pb[:], axis=AX.X, op=OP.add), cv)
                prog.wait("vector", cv)
                prog.on("vector", lambda e: e.reciprocal(rsum[:], sums[:]), cv)
                prog.wait("vector", cv)
                c0 = (NT1 - 1 - qt) * 128
                prog.on("vector", lambda e, hh=hh, c0=c0: e.scalar_tensor_tensor(
                    out=Pb[:], in0=Pb[:], scalar=rsum[:],
                    in1=B1b[:, hh, c0:c0 + S], op0=OP.mult, op1=OP.add), cv)
                prog.wait("tensor", cv)
                for t8 in range(4):
                    pb = t8 % 2
                    prog.wait("tensor", cv, ptr_free[pb])
                    for t in range(8):
                        tt = t8 * 8 + t
                        prog.on("tensor", lambda e, tt=tt, t=t, pb=pb: e.transpose(
                            ptr[pb][:, t * 128:(t + 1) * 128],
                            Pb[:, tt * 128:(tt + 1) * 128], ident[:]),
                            mm if t == 7 else None)
                    prog.wait("vector", mm)
                    prog.on("vector", lambda e, t8=t8, pb=pb: e.tensor_copy(
                        PT[:, t8 * 8:(t8 + 1) * 8, :],
                        ptr[pb].ap().rearrange("p (c f) -> p c f", c=8)), cv)
                    ptr_free[pb] = cv.n
                pb_free = mm.n
                prog.wait("tensor", cv)
                for tt in range(S // 128):
                    prog.on("tensor", lambda e, tt=tt, base=base: e.matmul(
                        pctx[base:base + 64, :], Vtm[:, tt, base:base + 64],
                        PT[:, tt, :], start=(tt == 0), stop=(tt == S // 128 - 1)),
                        mm if tt == S // 128 - 1 else None)
            prog.wait("vector", mm)
            prog.on("vector", lambda e, qt=qt, hp=hp: e.tensor_copy(
                ctxn[:, hp, qt * 128:(qt + 1) * 128], pctx[:]), cv)
            prog.wait("tensor", cv)

    prog.wait("tensor", cv)
    lctx = ExitStack()
    sbl = lambda n, shp, dt: lctx.enter_context(nc.sbuf_tensor(n, shp, dt))
    bns2 = sbl("bns2", [128, 12], F32)
    mv2 = sbl("mv2", [128, 2], F32)
    sd2 = sbl("sd2", [128, 1], F32)
    rstd2 = sbl("rstd2", [128, 1], F32)
    negmur2 = sbl("negmur2", [128, 1], F32)
    for qt in range(NT1):
        pb = qt % 2
        prog.wait("tensor", cv, ptr_free[pb])
        for ch in range(NCH):
            prog.on("tensor", lambda e, c=ch, qt=qt, pb=pb: e.transpose(
                ptr[pb][:, c * 128:(c + 1) * 128],
                ctxn[:, c, qt * 128:(qt + 1) * 128], ident[:]),
                mm if ch == NCH - 1 else None)
        prog.wait("vector", mm)
        prog.on("vector", lambda e, pb=pb: e.tensor_copy(
            ott.ap().rearrange("p (c f) -> p c f", c=NCH),
            ptr[pb].ap().rearrange("p (c f) -> p c f", c=NCH)), cv)
        ptr_free[pb] = cv.n
        prog.wait("vector", cv)
        prog.on("vector", lambda e: e.bn_stats(bns2[:, 0:6], ott[:, 0:512]))
        prog.on("vector", lambda e: e.bn_stats(bns2[:, 6:12], ott[:, 512:1024]), cv)
        prog.wait("vector", cv)
        prog.on("vector", lambda e: e.bn_aggr(mv2[:], bns2[:]), cv)
        prog.wait("scalar", cv)
        prog.on("scalar", lambda e: e.activation(out=sd2[:], in_=mv2[:, 1:2],
                                                 func=AF.Sqrt, bias=epsb[:], scale=1.0), ac)
        prog.wait("vector", ac)
        prog.on("vector", lambda e: e.reciprocal(rstd2[:], sd2[:]), cv)
        prog.wait("vector", cv)
        prog.on("vector", lambda e: e.tensor_scalar(
            out=negmur2[:], in0=mv2[:, 0:1], scalar1=rstd2[:], scalar2=-1.0,
            op0=OP.mult, op1=OP.mult), cv)
        prog.wait("vector", cv)
        prog.on("vector", lambda e: e.tensor_scalar(
            out=s1st[:], in0=ott[:], scalar1=rstd2[:], scalar2=negmur2[:],
            op0=OP.mult, op1=OP.add), cv)
        prog.wait("vector", cv)
        prog.on("vector", lambda e: e.tensor_tensor(
            out=s1st[:], in0=s1st[:], in1=gammab[:], op=OP.mult), cv)
        prog.wait("vector", cv)
        prog.on("vector", lambda e: e.tensor_tensor(
            out=s1st[:], in0=s1st[:], in1=betab[:], op=OP.add), cv)
        prog.wait("sync", cv)
        dma(prog, "sync", s1d[qt * 128:(qt + 1) * 128, :], s1st[:], st)
        prog.wait("vector", st)
    lctx.close()
    attn_cv = cv.n
    attn_st = st.n
    prog.wait("gpsimd", cv, attn_cv)
    prog.wait("gpsimd", st, attn_st)
    actx.close()

    fctx = ExitStack()
    sbf = lambda n, shp, dt: fctx.enter_context(nc.sbuf_tensor(n, shp, dt))
    wstrip = sbf("wstrip", [128, 4096], F32)
    w1blk = sbf("w1blk", [128, 2, NCH, 128], F32)
    wtmpA = sbf("wtmpA", [128, 1024], F32)
    wtmpS = sbf("wtmpS", [128, 1024], F32)
    w1tern = sbf("w1tern", [128, NCH, 128], BF16)
    w2tern = sbf("w2tern", [128, 32, 128], BF16)
    hbuf = sbf("hbuf", [128, 32, 1024], BF16)
    g1b = sbf("g1b", [128, NCH, 1024], BF16)
    g1t = sbf("g1t", [128, D], BF16)
    x1tile = sbf("x1tile", [128, D], F32)
    s1tile = sbf("s1tile", [128, D], F32)
    mterm = sbf("mterm", [128, D], BF16)
    colsum = sbf("colsum", [128, 1], F32)
    colacc = sbf("colacc", [128, 2], F32)
    tot256 = sbf("tot256", [1, 256], F32)
    tots = sbf("tots", [1, 2], F32)
    thr1 = sbf("thr1", [1, 2], F32)
    thrb = sbf("thrb", [128, 2], F32)
    csd = nc.dram_tensor("csd", [128, 2], F32, kind="Internal").ap()
    thrd = nc.dram_tensor("thrd", [1, 2], F32, kind="Internal").ap()

    for wi, (w_ap, nstr, width) in enumerate(((w1T, 8, 4096), (w2T, 32, 1024))):
        for srt in range(nstr):
            prog.wait("sync", cv)
            dma(prog, "sync", wstrip[:, 0:width],
                w_ap[srt * 128:(srt + 1) * 128, :], ldx)
            prog.wait("vector", ldx)
            prog.on("vector", lambda e, width=width: e.tensor_reduce(
                out=colsum[:], in_=wstrip[:, 0:width], axis=AX.X, op=OP.add,
                apply_absolute_value=True), cv)
            prog.wait("vector", cv)
            if srt == 0:
                prog.on("vector", lambda e, wi=wi: e.tensor_copy(
                    colacc[:, wi:wi + 1], colsum[:]), cv)
            else:
                prog.on("vector", lambda e, wi=wi: e.tensor_tensor(
                    out=colacc[:, wi:wi + 1], in0=colacc[:, wi:wi + 1],
                    in1=colsum[:], op=OP.add), cv)
    prog.wait("sync", cv)
    dma(prog, "sync", csd, colacc[:], st)
    prog.wait("sync", st)
    dma(prog, "sync", tot256[:], csd.rearrange("p c -> (p c)")[None, :], st)
    prog.wait("vector", st)
    for wi in range(2):
        prog.on("vector", lambda e, wi=wi: e.tensor_reduce(
            out=tots[0:1, wi:wi + 1],
            in_=tot256.ap().rearrange("q (p c) -> q c p", c=2)[:, wi, :],
            axis=AX.X, op=OP.add), cv)
    prog.wait("vector", cv)
    for wi in range(2):
        prog.on("vector", lambda e, wi=wi: e.tensor_scalar(
            out=thr1[0:1, wi:wi + 1], in0=tots[0:1, wi:wi + 1],
            scalar1=1.0 / (D * 4096), scalar2=1e-8, op0=OP.mult, op1=OP.max), cv)
    prog.wait("vector", cv)
    prog.on("vector", lambda e: e.tensor_scalar(
        out=thr1[:], in0=thr1[:], scalar1=0.5, scalar2=None, op0=OP.mult), cv)
    prog.wait("sync", cv)
    dma(prog, "sync", thrd, thr1[:], st)
    prog.wait("sync", st)
    from concourse.bass_types import AP as _AP
    dma(prog, "sync", thrb[:], _AP(tensor=thrd.tensor, offset=0,
                                   ap=[[0, 128], [1, 2]]), st)
    prog.wait("vector", st)
    prog.wait("scalar", st)
    prog.wait("tensor", st)

    def emit_tern(dst, srcap, wi, width):
        prog.wait("scalar", cv)
        prog.on("scalar", lambda e, srcap=srcap, width=width: e.activation(
            out=wtmpA[:, 0:width], in_=srcap, func=AF.Abs), ac)
        prog.on("scalar", lambda e, srcap=srcap, width=width: e.activation(
            out=wtmpS[:, 0:width], in_=srcap, func=AF.Sign), ac)
        prog.wait("vector", ac)
        prog.on("vector", lambda e, wi=wi, width=width: e.tensor_scalar(
            out=wtmpA[:, 0:width], in0=wtmpA[:, 0:width],
            scalar1=thrb[:, wi:wi + 1], scalar2=None, op0=OP.is_ge), cv)
        prog.wait("vector", cv)
        prog.on("vector", lambda e, dst=dst, width=width: e.tensor_tensor(
            out=dst, in0=wtmpS[:, 0:width], in1=wtmpA[:, 0:width],
            op=OP.mult), cv)
        prog.wait("tensor", cv)

    TCH = 1024
    w1blk_free = [0, 0]
    for tch in range(SPAN1 // TCH):
        t0c = qoff + tch * TCH
        for fc in range(32):
            wb = fc % 2
            prog.wait("sync", ac, w1blk_free[wb])
            dma(prog, "sync", w1blk[:, wb, :, :],
                w1T.rearrange("(c p) m -> p c m", p=128)[:, :, fc * 128:(fc + 1) * 128],
                ldx)
            prog.wait("scalar", ldx)
            emit_tern(w1tern.ap().rearrange("p c f -> p (c f)"),
                      w1blk.ap()[:, wb, :, :].rearrange("p c f -> p (c f)"), 0, 1024)
            w1blk_free[wb] = ac.n
            for half in range(2):
                bk = half
                prog.wait("tensor", cv, bank_free[bk])
                for kc in range(NCH):
                    prog.on("tensor", lambda e, kc=kc, half=half, bk=bk, t0c=t0c: e.matmul(
                        pscore[bk][:], w1tern[:, kc, :],
                        xn_fm[:, kc, t0c + half * 512: t0c + (half + 1) * 512],
                        start=(kc == 0), stop=(kc == NCH - 1)),
                        mm if kc == NCH - 1 else None)
                prog.wait("scalar", mm)
                prog.on("scalar", lambda e, fc=fc, half=half, bk=bk: e.activation(
                    out=hbuf[:, fc, half * 512:(half + 1) * 512], in_=pscore[bk][:],
                    func=AF.Gelu, bias=b1s[:, fc:fc + 1], scale=1.0), ac)
                prog.wait("vector", ac)
                prog.on("vector", lambda e: e.engine_nop() if hasattr(e, 'engine_nop') else e.nop(), cv)
                bank_free[bk] = cv.n
        for oc in range(NCH):
            prog.wait("sync", cv)
            dma(prog, "sync", wstrip.ap().rearrange("p (c m) -> p c m", c=32),
                w2T.rearrange("(c p) m -> p c m", p=128)[:, :, oc * 128:(oc + 1) * 128],
                ldx)
            prog.wait("scalar", ldx)
            for q4 in range(4):
                emit_tern(w2tern.ap()[:, q4 * 8:(q4 + 1) * 8, :].rearrange("p c f -> p (c f)"),
                          wstrip[:, q4 * 1024:(q4 + 1) * 1024], 1, 1024)
            for half in range(2):
                bk = 2 + half
                prog.wait("tensor", cv, bank_free[bk])
                for fc in range(32):
                    prog.on("tensor", lambda e, fc=fc, half=half, bk=bk: e.matmul(
                        pscore[bk][:], w2tern[:, fc, :],
                        hbuf[:, fc, half * 512:(half + 1) * 512],
                        start=(fc == 0), stop=(fc == 31)),
                        mm if fc == 31 else None)
                prog.wait("scalar", mm)
                prog.on("scalar", lambda e, oc=oc, half=half, bk=bk: e.activation(
                    out=g1b[:, oc, half * 512:(half + 1) * 512], in_=pscore[bk][:],
                    func=AF.Silu, bias=b2s[:, oc:oc + 1], scale=1.0), ac)
                prog.wait("vector", ac)
                prog.on("vector", lambda e: e.tensor_copy(colsum[:], colsum[:]), cv)
                bank_free[bk] = cv.n
        prog.wait("tensor", ac)
        for rt in range(TCH // 128):
            grow = tch * TCH + rt * 128
            pb = rt % 2
            prog.wait("tensor", cv, ptr_free[pb])
            for ch in range(NCH):
                prog.on("tensor", lambda e, c=ch, rt=rt, pb=pb: e.transpose(
                    ptr[pb][:, c * 128:(c + 1) * 128],
                    g1b[:, c, rt * 128:(rt + 1) * 128], ident[:]),
                    mm if ch == NCH - 1 else None)
            dma(prog, "sync", x1tile[:], x1f[qoff + grow: qoff + grow + 128, :], ldx)
            dma(prog, "sync", s1tile[:], s1d[grow: grow + 128, :], ldx)
            prog.wait("vector", mm)
            prog.wait("vector", ldx)
            prog.on("vector", lambda e, pb=pb: e.tensor_copy(
                g1t.ap().rearrange("p (c f) -> p c f", c=NCH),
                ptr[pb].ap().rearrange("p (c f) -> p c f", c=NCH)), cv)
            ptr_free[pb] = cv.n
            prog.wait("vector", cv)
            prog.on("vector", lambda e: e.tensor_tensor(
                out=mterm[:], in0=s1tile[:], in1=g1t[:], op=OP.mult), cv)
            prog.wait("vector", cv)
            prog.on("vector", lambda e: e.tensor_tensor(
                out=x1tile[:], in0=x1tile[:], in1=mterm[:], op=OP.add), cv)
            prog.wait("sync", cv)
            dma(prog, "sync", out_sp[grow: grow + 128, :], x1tile[:], st)
            prog.wait("vector", st)
            prog.wait("sync", st)

    prog.wait("sync", st)
    prog.wait("gpsimd", st)
    with nc.Block() as block:
        prog.finalize(block)
    fctx.close()
    gctx.close()
    ctx.close()
    return nc


def host_b_inputs(di, h, i):
    w1 = np.asarray(di["L0_ffn_W1"][i], np.float32)
    w2 = np.asarray(di["L0_ffn_W2"][i], np.float32)
    b1 = np.asarray(di["L0_ffn_b1"][i], np.float32)
    b2 = np.asarray(di["L0_ffn_b2"][i], np.float32)
    w1p = np.zeros((D, 22, 2, 128), np.float32)
    b1a = np.zeros((128, 22), np.float32)
    b1g = np.zeros((128, 22), np.float32)
    for c in range(22):
        wa = min(128, 2730 - c * 128)
        w1p[:, c, 0, :wa] = w1[:, c * 128: c * 128 + wa]
        w1p[:, c, 1, :wa] = w1[:, 2730 + c * 128: 2730 + c * 128 + wa]
        b1a[:wa, c] = b1[c * 128: c * 128 + wa]
        b1g[:wa, c] = b1[2730 + c * 128: 2730 + c * 128 + wa]
    w2p = np.zeros((2816, D), np.float32)
    w2p[:2730] = w2
    sel, bc = host_selbc(bc_scale=float(np.sqrt(8.0)))
    qs = np.asarray(di["L0_ffn_q_scale"][i], np.float32)
    ks = np.asarray(di["L0_ffn_k_scale"][i], np.float32)
    return {
        "wqkv": np.ascontiguousarray(di["L0_ffn_Wqkv"][i]),
        "wo_in": np.ascontiguousarray(di["L0_ffn_Wo"][i]),
        "w1p_in": w1p, "b1a_in": b1a, "b1g_in": b1g, "w2p_in": w2p,
        "b2_in": np.ascontiguousarray(b2.reshape(NCH, 128).T),
        "qs_in": np.ascontiguousarray(np.concatenate([qs, qs])[:, None]),
        "ks_in": np.ascontiguousarray(np.concatenate([ks, ks])[:, None]),
        "lng_in": np.ascontiguousarray(
            np.asarray(di["L0_ffn_ln_g"][i], np.float32).reshape(NCH, 128).T),
        "lnb_in": np.ascontiguousarray(
            np.asarray(di["L0_ffn_ln_b"][i], np.float32).reshape(NCH, 128).T),
        "tabs_in": np.ascontiguousarray(host_rotary(h).transpose(1, 0, 2)),
        "mask_in": np.ascontiguousarray(host_lmask().transpose(2, 0, 1, 3)),
        "ident_in": np.eye(128, dtype=np.float32),
        "perm_in": host_perm(),
        "sel_in": np.ascontiguousarray(sel.transpose(1, 0, 2)),
        "bc_in": np.ascontiguousarray(bc.transpose(1, 0, 2)),
    }


def host_c2_inputs(di, h):
    return {
        "wqT": np.ascontiguousarray(np.asarray(di["L1_attn_Wq"], np.float32).T),
        "wkT": np.ascontiguousarray(np.asarray(di["L1_attn_Wk"], np.float32).T),
        "wvT": np.ascontiguousarray(np.asarray(di["L1_attn_Wv"], np.float32).T),
        "bias1": host_bias_l1(np.asarray(di["L1_attn_bias_table"], np.float32), h),
        "gam_in": np.ascontiguousarray(
            np.asarray(di["L1_attn_ln_gamma"], np.float32)[None, :]),
        "bet_in": np.ascontiguousarray(
            np.asarray(di["L1_attn_ln_beta"], np.float32)[None, :]),
        "w1T": np.ascontiguousarray(np.asarray(di["L1_ffn_W1"], np.float32).T),
        "b1_in": np.ascontiguousarray(
            np.asarray(di["L1_ffn_b1"], np.float32).reshape(32, 128).T),
        "w2T": np.ascontiguousarray(np.asarray(di["L1_ffn_W2"], np.float32).T),
        "b2_in": np.ascontiguousarray(
            np.asarray(di["L1_ffn_b2"], np.float32).reshape(NCH, 128).T),
        "ident_in": np.eye(128, dtype=np.float32),
        "ones1_in": np.ones((1, 128), np.float32),
    }


def _fm_slice(x_fm_full, h):
    out = np.zeros((128, NCH, SPAN0 + 1), np.float32)
    lead = QS0[h] - 1
    if lead >= 0:
        out[:] = x_fm_full[:, :, lead:lead + SPAN0 + 1]
    else:
        out[:, :, 1:] = x_fm_full[:, :, 0:SPAN0]
    return np.ascontiguousarray(out)


def kernel(**inputs):
    x0_full = np.asarray(inputs["L0_attn_x"], np.float32)
    ident = np.eye(128, dtype=np.float32)

    s0 = np.empty((B, S, D), np.float32)
    xn_fm = [None] * B
    for h in range(2):
        nc_a = build_l0_attn(QS0[h])
        bias = host_bias_l0(np.asarray(inputs["L0_attn_rel_emb"], np.float32), h)
        in_maps = [{
            "x0": np.ascontiguousarray(x0_full[b]),
            "wq": inputs["L0_attn_wq"], "wk": inputs["L0_attn_wk"],
            "wv": inputs["L0_attn_wv"], "wo": inputs["L0_attn_wo"],
            "bias0": bias, "ident_in": ident} for b in range(B)]
        res = run_bass_kernel_spmd(nc_a, in_maps, core_ids=list(range(B)))
        for b in range(B):
            r = res.results[b]["s0_out"]
            if h == 0:
                s0[b, 0:SPAN0] = r
                xn_fm[b] = res.results[b]["xn_out"]
            else:
                s0[b, SPAN0:] = r[SPAN0 - (S - SPAN0):]

    x_fm = xn_fm
    nc_b = [build_l0ffn_layer(h) for h in range(2)]
    wd_b = [[host_b_inputs(inputs, h, i) for h in range(2)] for i in range(6)]
    for i in range(6):
        nxt = [np.empty((128, NCH, S), np.float32) for _ in range(B)]
        for h in range(2):
            in_maps = []
            for b in range(B):
                m = dict(wd_b[i][h])
                m["xin"] = _fm_slice(x_fm[b], h)
                in_maps.append(m)
            res = run_bass_kernel_spmd(nc_b[h], in_maps, core_ids=list(range(B)))
            for b in range(B):
                r = res.results[b]["xout"]
                if h == 0:
                    nxt[b][:, :, 0:SPAN0] = r
                else:
                    nxt[b][:, :, SPAN0:] = r[:, :, SPAN0 - (S - SPAN0):]
        x_fm = nxt

    nc_c1 = build_combine()
    in_maps = []
    for core in range(8):
        b, h = core // 2, core % 2
        in_maps.append({
            "x0sp": np.ascontiguousarray(x0_full[b, QS0[h]:QS0[h] + SPAN0]),
            "s0sp": np.ascontiguousarray(s0[b, QS0[h]:QS0[h] + SPAN0]),
            "g0fm": np.ascontiguousarray(x_fm[b][:, :, QS0[h]:QS0[h] + SPAN0]),
            "ident_in": ident,
        })
    res = run_bass_kernel_spmd(nc_c1, in_maps, core_ids=list(range(8)))
    x1 = np.empty((B, S, D), np.float32)
    for core in range(8):
        b, h = core // 2, core % 2
        r = res.results[core]["x1sp"]
        if h == 0:
            x1[b, 0:SPAN0] = r
        else:
            x1[b, SPAN0:] = r[SPAN0 - (S - SPAN0):]

    out = np.empty((B, S, D), np.float32)
    for h in range(2):
        nc_c2 = build_l1(QS1[h])
        wd = host_c2_inputs(inputs, h)
        in_maps = []
        for b in range(B):
            m = dict(wd)
            m["x1f"] = np.ascontiguousarray(x1[b])
            in_maps.append(m)
        res = run_bass_kernel_spmd(nc_c2, in_maps, core_ids=list(range(B)))
        for b in range(B):
            out[b, QS1[h]:QS1[h] + SPAN1] = res.results[b]["out_sp"]
    return out
